# version 14
# baseline (speedup 1.0000x reference)
import contextlib
import numpy as np
import ml_dtypes
import concourse.bass as bass
import concourse.mybir as mybir
from concourse.bass_utils import run_bass_kernel_spmd

F32 = mybir.dt.float32
BF16 = mybir.dt.bfloat16
AF = mybir.ActivationFunctionType
ALU = mybir.AluOpType
AX = mybir.AxisListType
f32 = np.float32
DBG = False

B, T, D, H, P = 2, 8192, 1024, 16, 64
TOK = 2048
NHP = 8
DCH = 8
E, FF = 8, 4096
EPS = 1e-5
CSHIFT = 60.0
NTILE = TOK // 128
NSEQ = T // 512
VROW = 65 * 64
TT = 1024


class KB:
    ENG_SEM = {"tensor": "sT", "vector": "sV", "scalar": "sA",
               "gpsimd": "sP", "sync": "sS"}

    def __init__(self, nc, stk):
        self.nc = nc
        self.sems = {}
        for n in ["sT", "sV", "sA", "sP", "sS", "qS", "qG"]:
            self.sems[n] = stk.enter_context(nc.semaphore(n))
        self.cnt = {n: 0 for n in self.sems}
        self.lastw = {}

    def w(self, eng, ek, deps):
        lw = self.lastw.setdefault(ek, {})
        for d in deps:
            if d is None:
                continue
            s, c = d
            if c <= 0 or lw.get(s, 0) >= c:
                continue
            eng.wait_ge(self.sems[s], c)
            lw[s] = c

    def op(self, eng, ek, fn, deps=(), inc=True):
        self.w(eng, ek, deps)
        inst = fn()
        if inc:
            s = self.ENG_SEM[ek]
            self.cnt[s] += 1
            inst.then_inc(self.sems[s], 1)
            return (s, self.cnt[s])
        return None

    def dma(self, eng, ek, q, out, in_, deps=()):
        self.w(eng, ek, deps)
        eng.dma_start(out=out, in_=in_).then_inc(self.sems[q], 16)
        self.cnt[q] += 16
        return (q, self.cnt[q])

    def snap(self):
        return [(n, c) for n, c in self.cnt.items()]


def build():
    nc = bass.Bass("TRN2", target_bir_lowering=False)
    xT = nc.dram_tensor("xT", [D, T], F32, kind="ExternalInput").ap()
    xown = nc.dram_tensor("xown", [TOK, D], F32, kind="ExternalInput").ap()
    wqkv = nc.dram_tensor("wqkv", [D, 768], F32, kind="ExternalInput").ap()
    bqkv = nc.dram_tensor("bqkv", [128, 6], F32, kind="ExternalInput").ap()
    gbc = nc.dram_tensor("gbc", [2, 128, 8], F32, kind="ExternalInput").ap()
    rw_d = nc.dram_tensor("rw_d", [128, 64], F32, kind="ExternalInput").ap()
    w1_d = nc.dram_tensor("w1_d", [E, D, FF], F32, kind="ExternalInput").ap()
    w2_d = nc.dram_tensor("w2_d", [E, FF, D], F32, kind="ExternalInput").ap()
    id32_d = nc.dram_tensor("id32_d", [128, 128], F32, kind="ExternalInput").ap()
    idbf_d = nc.dram_tensor("idbf_d", [128, 128], BF16, kind="ExternalInput").ap()
    out_d = nc.dram_tensor("out", [TOK, D], F32, kind="ExternalOutput").ap()
    akind = {"kind": "ExternalOutput"} if DBG else {}
    attn_scr = nc.dram_tensor("attn_scr", [4, T, P], F32, **akind).ap()
    if DBG:
        dbg_acc1 = nc.dram_tensor("dbg_acc1", [128, NTILE * D], F32, kind="ExternalOutput").ap()
        dbg_xnT = nc.dram_tensor("dbg_xnT", [128, DCH * TOK], BF16, kind="ExternalOutput").ap()
        dbg_wd = nc.dram_tensor("dbg_wd", [128, 128], F32, kind="ExternalOutput").ap()
        dbg_acc2 = nc.dram_tensor("dbg_acc2", [128, NTILE * D], F32, kind="ExternalOutput").ap()

    stk = contextlib.ExitStack()
    with stk:
        kb = KB(nc, stk)
        block = stk.enter_context(nc.Block())
        live, palive = {}, {}

        def alloc(name, shape, dt, side="right"):
            cm = nc.sbuf_tensor(name, shape, dt, side=side)
            t = cm.__enter__()
            live[name] = cm
            return t

        def free(*names):
            for n in names:
                live.pop(n).__exit__(None, None, None)

        def palloc(name, shape, dt):
            cm = nc.psum_tensor(name, shape, dt)
            t = cm.__enter__()
            palive[name] = cm
            return t

        def pfree(*names):
            for n in names:
                palive.pop(n).__exit__(None, None, None)

        def sec(ek, body):
            getattr(block, ek)(body)

        id32 = alloc("id32", [128, 128], F32, side="left")
        idbf = alloc("idbf", [128, 128], BF16, side="left")
        rwTh = alloc("rwTh", [128, 64], BF16, side="left")
        rwTl = alloc("rwTl", [128, 64], BF16, side="left")
        smc = alloc("smc", [128, 44], F32, side="left")

        wq = alloc("wq", [128, DCH * 768], BF16)
        xt2 = alloc("xt2", [128, 2 * DCH * 512], BF16)
        vcs = alloc("vcs", [128, 2 * 512], BF16)
        bqs = alloc("bqs", [128, 6], F32)
        rwTf = alloc("rwTf", [128, 64], F32)
        rwt1 = alloc("rwt1", [128, 64], F32)
        ktp = alloc("ktp", [128, 2 * T], BF16, side="left")
        qtp = alloc("qtp", [128, 2 * T], BF16, side="left")
        vtok = alloc("vtok", [128, 4 * VROW], BF16, side="left")

        psK = palloc("psK", [128, 2 * 512], F32)
        psV = palloc("psV", [128, 2 * 512], F32)
        psQ = palloc("psQ", [128, 2 * 512], F32)
        psVT = palloc("psVT", [128, 1024], BF16)

        tk = {}

        def s_prep(s):
            tk["id32"] = kb.dma(s, "sync", "qS", id32[:, :], id32_d[:, :])
            tk["idbf"] = kb.dma(s, "sync", "qS", idbf[:, :], idbf_d[:, :])
            tk["bqs"] = kb.dma(s, "sync", "qS", bqs[:, :], bqkv[:, :])
            tk["rwTf"] = kb.dma(s, "sync", "qS", rwTf[:, :], rw_d[:, :])
        sec("sync", s_prep)

        def g_prep(g):
            tk["ones"] = kb.op(g, "gpsimd", lambda: g.memset(
                vtok[:, :].rearrange("p (x m) -> p x m", m=65)[:, :, 64:65], 1.0))
            tk["wq"] = kb.dma(g, "gpsimd", "qG",
                              wq[:, :].rearrange("p (c m) -> p c m", c=DCH),
                              wqkv.rearrange("(c p) m -> p c m", p=128))
        sec("gpsimd", g_prep)

        def a_prep(s):
            kb.op(s, "scalar", lambda: s.activation(
                smc[:, 39:40], smc[:, 39:40], AF.Exp,
                bias=f32(0.0), scale=f32(0.0)), inc=True)
        sec("scalar", a_prep)

        def v_prep(v):
            t1 = kb.op(v, "vector", lambda: v.tensor_copy(rwTh[:, :], rwTf[:, :]),
                       deps=[tk["rwTf"]])
            t2 = kb.op(v, "vector", lambda: v.tensor_copy(rwt1[:, :], rwTh[:, :]),
                       deps=[t1])
            t3 = kb.op(v, "vector", lambda: v.tensor_sub(
                rwt1[:, :], rwTf[:, :], rwt1[:, :]), deps=[t2])
            tk["rwT"] = kb.op(v, "vector", lambda: v.tensor_copy(
                rwTl[:, :], rwt1[:, :]), deps=[t3])
        sec("vector", v_prep)

        xt_tok = {}
        xt_free = [None, None]
        psK_free = [None, None]
        psV_free = [None, None]
        psQ_free = [None, None]
        vcs_free = [None, None]
        psVT_free = None

        for tc in range(NSEQ):
            par = tc % 2

            def g_x(g, tc=tc, par=par):
                xt_tok[tc] = kb.dma(
                    g, "gpsimd", "qG",
                    xt2[:, 4096 * par:4096 * par + 4096].rearrange(
                        "p (c m) -> p c m", c=DCH),
                    xT[:, 512 * tc:512 * tc + 512].rearrange(
                        "(c p) m -> p c m", p=128),
                    deps=[xt_free[par]])
            sec("gpsimd", g_x)

            for pair in range(2):
                seq = tc * 2 + pair
                sp = seq % 2

                def t_kvq(t, tc=tc, pair=pair, sp=sp, par=par):
                    for d in range(DCH):
                        tkK = kb.op(t, "tensor", lambda d=d: t.matmul(
                            psK[:, 512 * sp:512 * sp + 512],
                            wq[:, 768 * d + 256 + 128 * pair:
                               768 * d + 256 + 128 * pair + 128],
                            xt2[:, 4096 * par + 512 * d:4096 * par + 512 * d + 512],
                            start=(d == 0), stop=(d == 7)),
                            deps=[xt_tok[tc], tk["wq"], psK_free[sp]],
                            inc=(d == 7))
                    tk["K"] = tkK
                    for d in range(DCH):
                        tkV = kb.op(t, "tensor", lambda d=d: t.matmul(
                            psV[:, 512 * sp:512 * sp + 512],
                            wq[:, 768 * d + 512 + 128 * pair:
                               768 * d + 512 + 128 * pair + 128],
                            xt2[:, 4096 * par + 512 * d:4096 * par + 512 * d + 512],
                            start=(d == 0), stop=(d == 7)),
                            deps=[psV_free[sp]], inc=(d == 7))
                    tk["V"] = tkV
                    for d in range(DCH):
                        tkQ = kb.op(t, "tensor", lambda d=d: t.matmul(
                            psQ[:, 512 * sp:512 * sp + 512],
                            wq[:, 768 * d + 128 * pair:768 * d + 128 * pair + 128],
                            xt2[:, 4096 * par + 512 * d:4096 * par + 512 * d + 512],
                            start=(d == 0), stop=(d == 7)),
                            deps=[psQ_free[sp]], inc=(d == 7))
                    tk["Q"] = tkQ
                sec("tensor", t_kvq)
                if pair == 1:
                    xt_free[par] = tk["Q"]

                def v_kvq(v, tc=tc, pair=pair, sp=sp):
                    tk["Ke"] = kb.op(v, "vector", lambda: v.tensor_scalar(
                        ktp[:, 8192 * pair + 512 * tc:8192 * pair + 512 * tc + 512],
                        psK[:, 512 * sp:512 * sp + 512],
                        bqs[:, 2 + pair:3 + pair], None, ALU.add),
                        deps=[tk["K"], tk["bqs"]])
                    tk["Ve"] = kb.op(v, "vector", lambda: v.tensor_scalar(
                        vcs[:, 512 * sp:512 * sp + 512],
                        psV[:, 512 * sp:512 * sp + 512],
                        bqs[:, 4 + pair:5 + pair], None, ALU.add),
                        deps=[tk["V"], vcs_free[sp]])
                    tk["Qe"] = kb.op(v, "vector", lambda: v.tensor_scalar(
                        qtp[:, 8192 * pair + 512 * tc:8192 * pair + 512 * tc + 512],
                        psQ[:, 512 * sp:512 * sp + 512],
                        bqs[:, pair:pair + 1], None, ALU.add),
                        deps=[tk["Q"], tk["bqs"]])
                sec("vector", v_kvq)
                psK_free[sp] = tk["Ke"]
                psV_free[sp] = tk["Ve"]
                psQ_free[sp] = tk["Qe"]

                def t_vt(t, sp=sp):
                    last = None
                    for j in range(4):
                        last = kb.op(t, "tensor", lambda j=j: t.transpose(
                            psVT[:, 128 * j:128 * j + 128],
                            vcs[:, 512 * sp + 128 * j:512 * sp + 128 * j + 128],
                            idbf[:, :]),
                            deps=[tk["Ve"], psVT_free, tk["idbf"]], inc=(j == 3))
                    tk["vt"] = last
                sec("tensor", t_vt)
                vcs_free[sp] = tk["vt"]

                def v_vt(v, tc=tc, pair=pair):
                    pv = psVT[:, 0:512].rearrange("p (j m) -> p j m", m=128)
                    last = None
                    for half in range(2):
                        hd = 2 * pair + half
                        dst = vtok[:, VROW * hd:VROW * hd + VROW].rearrange(
                            "p (kc m) -> p kc m", m=65)[:, 4 * tc:4 * tc + 4, 0:64]
                        last = kb.op(v, "vector", lambda hd=hd, half=half, dst=dst:
                                     v.tensor_copy(dst, pv[:, :, 64 * half:64 * half + 64]),
                                     deps=[tk["vt"], tk["ones"]], inc=(half == 1))
                    tk["vte"] = last
                sec("vector", v_vt)
                psVT_free = tk["vte"]

        barrierA = kb.snap()
        free("rwt1", "rwTf", "bqs", "vcs", "xt2", "wq")
        pfree("psVT", "psQ", "psV", "psK")

        ssb = alloc("ssb", [128, 4 * 1024], BF16)
        otr = alloc("otr", [128, 2 * 512], F32)
        ostg = alloc("ostg", [128, 4 * 256], F32)

        psS = [palloc("psS0", [128, 1024], F32), palloc("psS1", [128, 1024], F32)]
        psO = palloc("psO", [128, 1024], F32)
        psTR = palloc("psTR", [128, 1024], F32)

        ssb_free0 = [None, None]
        psO_free = None
        psTR_free = [None, None]
        ostg_free = [None, None]

        sT_att0 = kb.cnt["sT"]
        sA_att0 = kb.cnt["sA"]

        def BT(i):
            return sT_att0 + 200 * i

        def p_scorepair(i, kc):
            if kc == 0:
                v = 1
            elif kc == 1:
                v = 2
            elif kc % 2 == 0:
                v = 2 + 6 * ((kc - 2) // 2) + 3
            else:
                v = 2 + 6 * ((kc - 3) // 2) + 6
            return ("sT", BT(i) + v)

        def p_svh1(i, kc):
            if kc == 62:
                v = 190
            elif kc == 63:
                v = 192
            elif kc % 2 == 0:
                v = 2 + 6 * (kc // 2) + 2
            else:
                v = 2 + 6 * ((kc - 1) // 2) + 5
            return ("sT", BT(i) + v)

        def p_exp(i, kc):
            return ("sA", sA_att0 + 64 * i + kc + 1)

        for pair in range(2):
            def a_pair(s, pair=pair):
                for qp in range(16):
                    i = 16 * pair + qp
                    for kc in range(64):
                        if kc >= 4:
                            war = p_svh1(i, kc - 4)
                        elif i > 0:
                            war = p_svh1(i - 1, 60 + kc)
                        else:
                            war = None
                        tkE = kb.op(s, "scalar", lambda kc=kc: s.activation(
                            ssb[:, 1024 * (kc % 4):1024 * (kc % 4) + 1024],
                            psS[kc % 2][:, :], AF.Exp,
                            bias=f32(-CSHIFT), scale=f32(1.0)),
                            deps=[p_scorepair(i, kc), war])
                        assert tkE == p_exp(i, kc)
            sec("scalar", a_pair)

            kbase = 8192 * pair
            for qp in range(16):
                i = 16 * pair + qp

                def t_main(t, pair=pair, qp=qp, i=i, kbase=kbase):
                    def scores(kc):
                        for half in range(2):
                            tok = kb.op(t, "tensor",
                                lambda kc=kc, half=half: t.matmul(
                                    psS[kc % 2][:, 512 * half:512 * half + 512],
                                    ktp[64 * half:64 * half + 64,
                                        kbase + 128 * kc:kbase + 128 * kc + 128],
                                    qtp[64 * half:64 * half + 64,
                                        kbase + 512 * qp:kbase + 512 * qp + 512],
                                    start=True, stop=True,
                                    tile_position=(64 * half, 0)),
                                deps=[p_exp(i, kc - 2) if kc >= 2 else
                                      (p_exp(i - 1, 62 if kc % 2 == 0 else 63)
                                       if i > 0 else None)] +
                                     (barrierA if i == 0 else []),
                                inc=(half == 1))
                            if half == 1:
                                assert tok == p_scorepair(i, kc), (i, kc, tok)

                    def sv(kc):
                        for half in range(2):
                            hl = 2 * pair + half
                            tok = kb.op(t, "tensor",
                                lambda kc=kc, half=half, hl=hl: t.matmul(
                                    psO[0:65, 512 * half:512 * half + 512],
                                    vtok[:, VROW * hl + 65 * kc:
                                         VROW * hl + 65 * kc + 65],
                                    ssb[:, 1024 * (kc % 4) + 512 * half:
                                        1024 * (kc % 4) + 512 * half + 512],
                                    start=(kc == 0), stop=(kc == 63)),
                                deps=[p_exp(i, kc),
                                      psO_free if kc == 0 else None],
                                inc=True)
                            if half == 1:
                                assert tok == p_svh1(i, kc), (i, kc, tok)

                    scores(0)
                    scores(1)
                    for kc2 in range(2, 64, 2):
                        sv(kc2 - 2)
                        scores(kc2)
                        sv(kc2 - 1)
                        scores(kc2 + 1)
                    sv(62)
                    sv(63)
                sec("tensor", t_main)

                def v_oev(v, i=i):
                    tk["oev"] = kb.op(v, "vector", lambda: v.tensor_copy(
                        otr[0:65, :], psO[0:65, :]),
                        deps=[p_svh1(i, 63)])
                sec("vector", v_oev)
                psO_free = tk["oev"]

                for j in range(8):
                    half, t2 = j // 4, j % 4
                    hl = 2 * pair + half
                    osl = 2 * (qp % 2) + half

                    def t_tr(t, half=half, t2=t2, j=j):
                        tk["atr"] = kb.op(t, "tensor", lambda: t.transpose(
                            psTR[:, 512 * (j % 2):512 * (j % 2) + 65],
                            otr[0:65, 512 * half + 128 * t2:
                                512 * half + 128 * t2 + 128],
                            id32[0:65, 0:65]),
                            deps=[tk["oev"], psTR_free[j % 2]])
                    sec("tensor", t_tr)

                    def v_nrm(v, t2=t2, osl=osl, j=j, qp=qp, half=half):
                        rz = kb.op(v, "vector", lambda: v.reciprocal(
                            smc[:, 38:39],
                            psTR[:, 512 * (j % 2) + 64:512 * (j % 2) + 65]),
                            deps=[tk["atr"]])
                        tk["nrm"] = kb.op(v, "vector", lambda: v.tensor_scalar(
                            ostg[:, 256 * osl + 64 * t2:256 * osl + 64 * t2 + 64],
                            psTR[:, 512 * (j % 2):512 * (j % 2) + 64],
                            smc[:, 38:39], None, ALU.mult),
                            deps=[rz, ostg_free[qp % 2] if j == 0 else None])
                    sec("vector", v_nrm)
                    psTR_free[j % 2] = tk["nrm"]

                    if t2 == 3:
                        def s_ost(s, hl=hl, qp=qp, osl=osl):
                            ostg_free[qp % 2] = kb.dma(
                                s, "sync", "qS",
                                attn_scr[hl, 512 * qp:512 * qp + 512, :].rearrange(
                                    "(t p) m -> p t m", p=128),
                                ostg[:, 256 * osl:256 * osl + 256].rearrange(
                                    "p (t m) -> p t m", m=64),
                                deps=[tk["nrm"]])
                        sec("sync", s_ost)
                assert kb.cnt["sT"] == BT(i) + 200, (i, kb.cnt["sT"])

        barrierB = kb.snap()
        free("ostg", "otr", "ssb")
        free("vtok", "qtp", "ktp")
        pfree("psTR", "psO", "psS1", "psS0")

        acc = alloc("acc", [128, NTILE * D], F32, side="left")
        xnT = alloc("xnT", [128, DCH * TOK], BF16, side="left")
        wd = alloc("wd", [128, 128], F32, side="left")
        gbs = alloc("gbs", [128, 16], F32)
        xti = alloc("xti", [128, 2 * D], F32)
        ati = alloc("ati", [128, 2 * D], F32)
        v1t = alloc("v1t", [128, 2 * D], F32)
        xct = alloc("xct", [128, 2 * D], F32)
        x1t = alloc("x1t", [128, 2 * D], F32)
        xwt = alloc("xwt", [128, 2 * D], F32)
        sqd = alloc("sqd", [128, D], F32)
        lgb = alloc("lgb", [128, 2 * 32], F32)

        psL = palloc("psL", [128, 512], F32)
        psX = palloc("psX", [128, 1024], F32)

        def s_gb(s):
            tk["gb0"] = kb.dma(s, "sync", "qS", gbs[:, 0:8], gbc[0, :, :],
                               deps=barrierB)
            tk["gb1"] = kb.dma(s, "sync", "qS", gbs[:, 8:16], gbc[1, :, :])
        sec("sync", s_gb)

        xld = {}
        ald = {}

        sVC0 = kb.cnt["sV"]
        sAC0 = kb.cnt["sA"]
        sTC0 = kb.cnt["sT"]

        def pV(t0, k):
            return ("sV", sVC0 + 24 * t0 + k)

        def pA(t0, k):
            return ("sA", sAC0 + 7 * t0 + k)

        def pT(t0, k):
            return ("sT", sTC0 + 2 * t0 + k)

        def a_c(s):
            for t0 in range(NTILE):
                pt = t0 % 2
                c1, c2 = 12 * pt, 12 * pt + 6
                xc = xct[:, D * pt:D * pt + D]
                for (cc, src_k) in ((c1, 4), (c2, 9)):
                    tok = kb.op(s, "scalar", lambda cc=cc, xc=xc: s.activation(
                        sqd[:, :], xc, AF.Square,
                        accum_out=smc[:, cc + 2:cc + 3]), deps=[pV(t0, src_k)])
                    tok = kb.op(s, "scalar", lambda cc=cc: s.activation(
                        smc[:, cc + 4:cc + 5], smc[:, cc + 2:cc + 3], AF.Ln,
                        bias=f32(EPS), scale=f32(1.0 / D)), deps=[tok])
                    tok = kb.op(s, "scalar", lambda cc=cc: s.activation(
                        smc[:, cc + 5:cc + 6], smc[:, cc + 4:cc + 5], AF.Exp,
                        bias=f32(0.0), scale=f32(-0.5)), deps=[tok])
                cr = 24 + 4 * pt
                tok = kb.op(s, "scalar", lambda pt=pt, cr=cr: s.activation(
                    lgb[:, 32 * pt + 8:32 * pt + 16], lgb[:, 32 * pt:32 * pt + 8],
                    AF.Exp, bias=smc[:, cr + 1:cr + 2], scale=f32(1.0 / 1.5)),
                    deps=[pV(t0, 14)])
                assert tok == pA(t0, 7)
        sec("scalar", a_c)

        def s_x(s, t0):
            xld[t0] = kb.dma(s, "sync", "qS",
                             xti[:, D * (t0 % 2):D * (t0 % 2) + D],
                             xown[128 * t0:128 * t0 + 128, :],
                             deps=(barrierB if t0 < 2 else [pV(t0 - 2, 4)]))

        def s_at(s, t0):
            hl, a0 = t0 // 4, 128 * (t0 % 4)
            src = attn_scr[hl].rearrange("(a r) m -> a (r m)", r=16)[a0:a0 + 128, :]
            ald[t0] = kb.dma(s, "sync", "qS",
                             ati[:, D * (t0 % 2):D * (t0 % 2) + D], src,
                             deps=(barrierB if t0 < 2 else [pV(t0 - 2, 1)]))
        sec("sync", lambda s: s_x(s, 0))
        sec("sync", lambda s: s_at(s, 0))
        sec("sync", lambda s: s_x(s, 1))
        sec("sync", lambda s: s_at(s, 1))

        for t0 in range(NTILE):
            pt = t0 % 2
            c1, c2 = 12 * pt, 12 * pt + 6
            cr = 24 + 4 * pt
            xsl = xti[:, D * pt:D * pt + D]
            asl = ati[:, D * pt:D * pt + D]
            v1 = v1t[:, D * pt:D * pt + D]
            xc = xct[:, D * pt:D * pt + D]
            x1 = x1t[:, D * pt:D * pt + D]
            xw = xwt[:, D * pt:D * pt + D]
            lgs = lgb[:, 32 * pt:32 * pt + 32]

            def v_a(v, t0=t0, pt=pt, c1=c1, c2=c2):
                tok = kb.op(v, "vector", lambda: v.tensor_add(v1, xsl, asl),
                            deps=[xld[t0], ald[t0]] +
                                 ([pV(t0 - 2, 4)] if t0 >= 2 else []))
                assert tok == pV(t0, 1)
                kb.op(v, "vector", lambda: v.reduce_sum(
                    smc[:, c1:c1 + 1], v1, axis=AX.X), deps=[tok])
                kb.op(v, "vector", lambda: v.tensor_scalar(
                    smc[:, c1 + 1:c1 + 2], smc[:, c1:c1 + 1],
                    -1.0 / D, None, ALU.mult), deps=[pV(t0, 2)])
                kb.op(v, "vector", lambda: v.tensor_scalar(
                    xc, v1, smc[:, c1 + 1:c1 + 2], None, ALU.add),
                    deps=[pV(t0, 3)] + ([pA(t0 - 2, 4)] if t0 >= 2 else []))
                kb.op(v, "vector", lambda: v.tensor_scalar(
                    x1, xc, smc[:, c1 + 5:c1 + 6], None, ALU.mult),
                    deps=[pA(t0, 3)])
                kb.op(v, "vector", lambda t0=t0: v.tensor_scalar(
                    acc[:, 1024 * t0:1024 * t0 + 1024], x1,
                    2.0, None, ALU.mult), deps=[pV(t0, 5)])
                kb.op(v, "vector", lambda: v.reduce_sum(
                    smc[:, c2:c2 + 1], x1, axis=AX.X), deps=[pV(t0, 5)])
                kb.op(v, "vector", lambda: v.tensor_scalar(
                    smc[:, c2 + 1:c2 + 2], smc[:, c2:c2 + 1],
                    -1.0 / D, None, ALU.mult), deps=[pV(t0, 7)])
                kb.op(v, "vector", lambda: v.tensor_scalar(
                    xc, x1, smc[:, c2 + 1:c2 + 2], None, ALU.add),
                    deps=[pV(t0, 8), pA(t0, 1)])
                kb.op(v, "vector", lambda: v.tensor_scalar(
                    xw, xc, smc[:, c2 + 5:c2 + 6], None, ALU.mult),
                    deps=[pA(t0, 6)] + ([pT(t0 - 2, 1)] if t0 >= 2 else []))
            sec("vector", v_a)

            def t_xnt(t, t0=t0, xw=xw):
                last = None
                for d in range(DCH):
                    last = kb.op(t, "tensor", lambda d=d: t.transpose(
                        psX[:, 128 * d:128 * d + 128],
                        xw[:, 128 * d:128 * d + 128], id32[:, :]),
                        deps=[pV(t0, 10)] +
                             ([pV(t0 - 1, 11)] if t0 >= 1 else []),
                        inc=(d == 7))
                assert last == pT(t0, 1)
            sec("tensor", t_xnt)

            def v_b(v, t0=t0):
                last = None
                for d in range(DCH):
                    last = kb.op(v, "vector", lambda d=d: v.tensor_scalar(
                        xnT[:, 2048 * d + 128 * t0:2048 * d + 128 * t0 + 128],
                        psX[:, 128 * d:128 * d + 128],
                        gbs[:, d:d + 1], gbs[:, 8 + d:9 + d],
                        ALU.mult, ALU.add),
                        deps=[pT(t0, 1), tk["gb0"], tk["gb1"]], inc=(d == 7))
                assert last == pV(t0, 11)
            sec("vector", v_b)

            def t_rt(t, t0=t0):
                n = 0
                for d in range(DCH):
                    for hl2 in range(2):
                        rsrc = rwTh if hl2 == 0 else rwTl
                        last = kb.op(t, "tensor", lambda d=d, rsrc=rsrc, n=n:
                                     t.matmul(
                                         psL[:, 0:8],
                                         xnT[:, 2048 * d + 128 * t0:
                                             2048 * d + 128 * t0 + 128],
                                         rsrc[:, 8 * d:8 * d + 8],
                                         start=(n == 0), stop=(n == 15)),
                                     deps=[pV(t0, 11), tk["rwT"]] +
                                          ([pV(t0 - 1, 12)] if t0 >= 1 else []),
                                     inc=(n == 15))
                        n += 1
                assert last == pT(t0, 2)
            sec("tensor", t_rt)

            def v_c(v, t0=t0, pt=pt, cr=cr, lgs=lgs):
                tok = kb.op(v, "vector", lambda: v.tensor_copy(
                    lgs[:, 0:8], psL[:, 0:8]), deps=[pT(t0, 2)])
                assert tok == pV(t0, 12)
                kb.op(v, "vector", lambda: v.reduce_max(
                    smc[:, cr:cr + 1], lgs[:, 0:8], axis=AX.X), deps=[tok])
                kb.op(v, "vector", lambda: v.tensor_scalar(
                    smc[:, cr + 1:cr + 2], smc[:, cr:cr + 1],
                    -1.0 / 1.5, None, ALU.mult), deps=[pV(t0, 13)])
                kb.op(v, "vector", lambda: v.reduce_sum(
                    smc[:, cr + 2:cr + 3], lgs[:, 8:16], axis=AX.X),
                    deps=[pA(t0, 7)])
                kb.op(v, "vector", lambda: v.reciprocal(
                    smc[:, cr + 3:cr + 4], smc[:, cr + 2:cr + 3]),
                    deps=[pV(t0, 15)])
                kb.op(v, "vector", lambda: v.tensor_scalar(
                    lgs[:, 16:24], lgs[:, 8:16], smc[:, cr + 3:cr + 4],
                    None, ALU.mult), deps=[pV(t0, 16)])
                kb.op(v, "vector", lambda: v.reduce_max(
                    smc[:, cr + 1:cr + 2], lgs[:, 16:24], axis=AX.X),
                    deps=[pV(t0, 17)])
                kb.op(v, "vector", lambda: v.tensor_scalar(
                    lgs[:, 24:32], lgs[:, 16:24], smc[:, cr + 1:cr + 2],
                    None, ALU.is_equal), deps=[pV(t0, 18)])
                kb.op(v, "vector", lambda: v.tensor_scalar(
                    lgs[:, 24:32], lgs[:, 24:32], 1e30, None, ALU.mult),
                    deps=[pV(t0, 19)])
                kb.op(v, "vector", lambda: v.tensor_sub(
                    lgs[:, 24:32], lgs[:, 16:24], lgs[:, 24:32]),
                    deps=[pV(t0, 20)])
                kb.op(v, "vector", lambda: v.reduce_max(
                    smc[:, cr + 1:cr + 2], lgs[:, 24:32], axis=AX.X),
                    deps=[pV(t0, 21)])
                kb.op(v, "vector", lambda: v.tensor_scalar(
                    lgs[:, 24:32], lgs[:, 16:24], smc[:, cr + 1:cr + 2],
                    None, ALU.is_ge), deps=[pV(t0, 22)])
                tok = kb.op(v, "vector", lambda t0=t0: v.tensor_mul(
                    wd[:, 8 * t0:8 * t0 + 8], lgs[:, 16:24], lgs[:, 24:32]),
                    deps=[pV(t0, 23)])
                assert tok == pV(t0, 24)
            sec("vector", v_c)

            if t0 + 2 < NTILE:
                sec("sync", lambda s, t0=t0: s_x(s, t0 + 2))
                sec("sync", lambda s, t0=t0: s_at(s, t0 + 2))

        barrierC = kb.snap()
        if DBG:
            def s_dc(s):
                kb.dma(s, "sync", "qS", dbg_acc1, acc[:, :], deps=barrierC)
                kb.dma(s, "sync", "qS", dbg_xnT, xnT[:, :])
                kb.dma(s, "sync", "qS", dbg_wd, wd[:, :])
            sec("sync", s_dc)
            barrierC = kb.snap()
        free("lgb", "sqd", "xwt", "x1t", "xct", "v1t", "ati", "xti", "gbs")
        pfree("psX", "psL")

        w1s = alloc("w1s", [128, 2 * DCH * 256], BF16)
        w2s = alloc("w2s", [128, 2 * 32 * 128], BF16)
        hT = alloc("hT", [128, 32 * TT], BF16)
        ysc = alloc("ysc", [128, 2 * 128], F32)

        psh = palloc("psh", [128, 4 * 512], F32)
        psy = palloc("psy", [128, 2 * 512], F32)

        NSLAB = 2 * E * 16
        NQ = 2 * E * 8
        w1_tok = {}
        w2_tok = {}
        acc_tok = {}

        sTD0 = kb.cnt["sT"]
        sAD0 = kb.cnt["sA"]
        sVD0 = kb.cnt["sV"]

        def sT_base(eidx):
            return sTD0 + 128 * eidx

        def sA_base(eidx):
            return sAD0 + 128 * eidx + (24 if eidx > 7 else 0)

        def sV_base(eidx):
            return sVD0 + 64 * eidx + (32 if eidx > 7 else 0)

        def silu_last_of_slab(gi):
            eidx, fs = gi // 16, gi % 16
            return ("sA", sA_base(eidx) + 4 * (fs + 1))

        def y_last_of_quarter(qi):
            eidx, dq = qi // 8, qi % 8
            return ("sT", sT_base(eidx) + 64 + 8 * (dq + 1))

        def issue_weights(g, eidx):
            if eidx >= 16:
                return
            for fs in range(16):
                gi = eidx * 16 + fs

                def do1(gi=gi, e=(gi // 16) % E if False else None):
                    pass
                e_ = ((gi // 16) % E) if False else ((gi % 256) // 16) % E
                e_ = (gi // 16) % E
                w1_tok[gi] = kb.dma(
                    g, "gpsimd", "qG",
                    w1s[:, 2048 * (gi % 2):2048 * (gi % 2) + 2048].rearrange(
                        "p (c m) -> p c m", c=DCH),
                    w1_d[e_, :, 256 * fs:256 * fs + 256].rearrange(
                        "(c p) m -> p c m", p=128),
                    deps=[silu_last_of_slab(gi - 2) if gi >= 2 else None] +
                         (barrierC if gi < 2 else []))
            for dq in range(8):
                qi = eidx * 8 + dq
                e_ = (qi // 8) % E
                w2_tok[qi] = kb.dma(
                    g, "gpsimd", "qG",
                    w2s[:, 4096 * (qi % 2):4096 * (qi % 2) + 4096].rearrange(
                        "p (c m) -> p c m", c=32),
                    w2_d[e_, :, 128 * dq:128 * dq + 128].rearrange(
                        "(c p) m -> p c m", p=128),
                    deps=[y_last_of_quarter(qi - 2) if qi >= 2 else None] +
                         (barrierC if qi < 2 else []))

        sec("gpsimd", lambda g: issue_weights(g, 0))
        sec("gpsimd", lambda g: issue_weights(g, 1))

        xct2 = alloc("xct2", [128, D], F32)
        sqd2 = alloc("sqd2", [128, D], F32)
        ot = alloc("ot", [128, 2 * D], F32)
        e_state = {"echain": None, "st_free": [None, None], "final": None}

        def emit_E_half(h):
            for t0 in range(8 * h, 8 * h + 8):
                asl = acc[:, 1024 * t0:1024 * t0 + 1024]
                toks = {}

                def v_e1(v, t0=t0, asl=asl):
                    d0 = [e_state["echain"]] + \
                         [acc_tok[(t0, dqq)] for dqq in range(8)]
                    toks["s"] = kb.op(v, "vector", lambda: v.reduce_sum(
                        smc[:, 32:33], asl, axis=AX.X), deps=d0)
                    toks["nm"] = kb.op(v, "vector", lambda: v.tensor_scalar(
                        smc[:, 33:34], smc[:, 32:33], -1.0 / D, None, ALU.mult),
                        deps=[toks["s"]])
                    toks["xc"] = kb.op(v, "vector", lambda: v.tensor_scalar(
                        xct2[:, :], asl, smc[:, 33:34], None, ALU.add),
                        deps=[toks["nm"]])
                sec("vector", v_e1)

                def a_e(s, t0=t0):
                    toks["sq"] = kb.op(s, "scalar", lambda: s.activation(
                        sqd2[:, :], xct2[:, :], AF.Square,
                        accum_out=smc[:, 34:35]), deps=[toks["xc"]])
                    toks["ln"] = kb.op(s, "scalar", lambda: s.activation(
                        smc[:, 35:36], smc[:, 34:35], AF.Ln,
                        bias=f32(EPS), scale=f32(1.0 / D)), deps=[toks["sq"]])
                    toks["rstd"] = kb.op(s, "scalar", lambda: s.activation(
                        smc[:, 36:37], smc[:, 35:36], AF.Exp,
                        bias=f32(0.0), scale=f32(-0.5)), deps=[toks["ln"]])
                sec("scalar", a_e)

                def v_e2(v, t0=t0):
                    toks["o"] = kb.op(v, "vector", lambda: v.tensor_scalar(
                        ot[:, D * (t0 % 2):D * (t0 % 2) + D], xct2[:, :],
                        smc[:, 36:37], None, ALU.mult),
                        deps=[toks["rstd"], e_state["st_free"][t0 % 2]])
                sec("vector", v_e2)
                e_state["echain"] = toks["o"]

                def s_o(s, t0=t0):
                    tokd = kb.dma(s, "sync", "qS",
                                  out_d[128 * t0:128 * t0 + 128, :],
                                  ot[:, D * (t0 % 2):D * (t0 % 2) + D],
                                  deps=[toks["o"]])
                    e_state["st_free"][t0 % 2] = tokd
                    e_state["final"] = tokd
                sec("sync", s_o)

        for tt in range(2):
            for e in range(E):
                eidx = tt * E + e
                sT0 = kb.cnt["sT"]
                sA0 = kb.cnt["sA"]
                sV0 = kb.cnt["sV"]
                assert sT0 == sT_base(eidx) and sA0 == sA_base(eidx) \
                    and sV0 == sV_base(eidx), (eidx, sT0, sA0, sV0)

                if eidx + 2 < 16:
                    sec("gpsimd", lambda g, eidx=eidx: issue_weights(g, eidx + 2))

                def a_blk(s, tt=tt, e=e, sT0=sT0, sV0=sV0):
                    for grp in range(64):
                        fs, fc, sub = grp // 4, (grp % 4) // 2, grp % 2
                        fcg = 2 * fs + fc
                        slot = grp % 4
                        kb.op(s, "scalar", lambda fcg=fcg, sub=sub, slot=slot:
                              s.activation(
                                  hT[:, 1024 * fcg + 512 * sub:
                                     1024 * fcg + 512 * sub + 512],
                                  psh[:, 512 * slot:512 * slot + 512], AF.Silu),
                              deps=[("sT", sT0 + grp + 1), ("sT", sT0)])
                    for k in range(64):
                        dq, tkc = k // 8, k % 8
                        ti = 8 * tt + tkc
                        slot = k % 2
                        kb.op(s, "scalar", lambda ti=ti, e=e, slot=slot, k=k:
                              s.activation(
                                  ysc[:, 128 * slot:128 * slot + 128],
                                  psy[:, 512 * slot:512 * slot + 128], AF.Copy,
                                  bias=0.0, scale=wd[:, 8 * ti + e:8 * ti + e + 1]),
                              deps=[("sT", sT0 + 64 + k + 1),
                                    ("sV", sV0 + k - 1) if k >= 2 else ("sV", sV0)])
                sec("scalar", a_blk)
                assert kb.cnt["sA"] == sA0 + 128

                def t_blk(t, tt=tt, e=e, eidx=eidx, sA0=sA0):
                    for grp in range(64):
                        fs, fc, sub = grp // 4, (grp % 4) // 2, grp % 2
                        gi = eidx * 16 + fs
                        slot = grp % 4
                        for d in range(DCH):
                            kb.op(t, "tensor", lambda gi=gi, fc=fc, sub=sub,
                                  slot=slot, d=d, tt=tt: t.matmul(
                                      psh[:, 512 * slot:512 * slot + 512],
                                      w1s[:, 2048 * (gi % 2) + 256 * d + 128 * fc:
                                          2048 * (gi % 2) + 256 * d + 128 * fc + 128],
                                      xnT[:, 2048 * d + 1024 * tt + 512 * sub:
                                          2048 * d + 1024 * tt + 512 * sub + 512],
                                      start=(d == 0), stop=(d == 7)),
                                  deps=[w1_tok[gi],
                                        ("sA", sA0 + grp - 3) if grp >= 4 else None],
                                  inc=(d == 7))
                    for k in range(64):
                        dq, tkc = k // 8, k % 8
                        qi = eidx * 8 + dq
                        slot = k % 2
                        for fcg in range(32):
                            kb.op(t, "tensor", lambda qi=qi, tkc=tkc, slot=slot,
                                  fcg=fcg: t.matmul(
                                      psy[:, 512 * slot:512 * slot + 128],
                                      hT[:, 1024 * fcg + 128 * tkc:
                                         1024 * fcg + 128 * tkc + 128],
                                      w2s[:, 4096 * (qi % 2) + 128 * fcg:
                                          4096 * (qi % 2) + 128 * fcg + 128],
                                      start=(fcg == 0), stop=(fcg == 31)),
                                  deps=[w2_tok[qi], ("sA", sA0 + 64),
                                        ("sA", sA0 + 64 + k - 1) if k >= 2 else None],
                                  inc=(fcg == 31))
                sec("tensor", t_blk)
                assert kb.cnt["sT"] == sT0 + 128

                def v_blk(v, tt=tt, sA0=sA0):
                    for k in range(64):
                        dq, tkc = k // 8, k % 8
                        ti = 8 * tt + tkc
                        slot = k % 2
                        a0 = 1024 * ti + 128 * dq
                        prev = acc_tok.get((ti, dq))
                        acc_tok[(ti, dq)] = kb.op(
                            v, "vector", lambda a0=a0, slot=slot: v.tensor_add(
                                acc[:, a0:a0 + 128], acc[:, a0:a0 + 128],
                                ysc[:, 128 * slot:128 * slot + 128]),
                            deps=[("sA", sA0 + 64 + k + 1), prev] +
                                 ([] if prev else barrierC))
                sec("vector", v_blk)
                assert kb.cnt["sV"] == sV0 + 64
                if eidx == 7:
                    emit_E_half(0)
                elif eidx == 15:
                    emit_E_half(1)

        barrierD = kb.snap()
        if DBG:
            sec("sync", lambda s: kb.dma(s, "sync", "qS", dbg_acc2, acc[:, :],
                                         deps=barrierD))
            barrierD = kb.snap()

        def s_fin(s):
            kb.w(s, "sync", [e_state["final"]] + barrierD)
        sec("sync", s_fin)

        free("ot", "sqd2", "xct2")
        free("ysc", "hT", "w2s", "w1s")
        free("wd", "xnT")
        pfree("psy", "psh")
        free("acc")
        free("smc", "rwTl", "rwTh", "idbf", "id32")

    return nc


_NC_CACHE = {}


def kernel(L0_attn_x, L0_attn_W_qkv, L0_attn_b_qkv, L0_ffn_gamma, L0_ffn_beta,
           L0_ffn_router_w, L0_ffn_W1, L0_ffn_W2):
    x = np.ascontiguousarray(np.asarray(L0_attn_x, dtype=np.float32))
    wqkv_full = np.asarray(L0_attn_W_qkv, dtype=np.float32)
    bq_full = np.asarray(L0_attn_b_qkv, dtype=np.float32)
    gamma = np.asarray(L0_ffn_gamma, dtype=np.float32)
    beta = np.asarray(L0_ffn_beta, dtype=np.float32)
    rw = np.asarray(L0_ffn_router_w, dtype=np.float32)
    rwt = np.ascontiguousarray(
        rw.reshape(8, 8, 128).transpose(2, 1, 0).reshape(128, 64))
    w1 = np.ascontiguousarray(np.asarray(L0_ffn_W1, dtype=np.float32))
    w2 = np.ascontiguousarray(np.asarray(L0_ffn_W2, dtype=np.float32))

    gbc = np.ascontiguousarray(np.stack([
        gamma.reshape(8, 128).T, beta.reshape(8, 128).T]))
    id32 = np.eye(128, dtype=np.float32)
    idbf = np.eye(128, dtype=ml_dtypes.bfloat16)

    if "nc" not in _NC_CACHE:
        _NC_CACHE["nc"] = build()
    nc = _NC_CACHE["nc"]

    in_maps = []
    xTb = [np.ascontiguousarray(x[b].T) for b in range(B)]
    for c in range(8):
        b, g = c // 4, c % 4
        cols = list(range(256 * g, 256 * g + 256)) + \
               list(range(1024 + 256 * g, 1024 + 256 * g + 256)) + \
               list(range(2048 + 256 * g, 2048 + 256 * g + 256))
        wq_c = np.ascontiguousarray(wqkv_full[:, cols])
        bq_c = np.ascontiguousarray(
            bq_full[cols].reshape(6, 128).T)
        in_maps.append({
            "xT": xTb[b],
            "xown": np.ascontiguousarray(x[b, 2048 * g:2048 * g + 2048, :]),
            "wqkv": wq_c, "bqkv": bq_c, "gbc": gbc, "rw_d": rwt,
            "w1_d": w1, "w2_d": w2, "id32_d": id32, "idbf_d": idbf,
        })
    res = run_bass_kernel_spmd(nc, in_maps, core_ids=list(range(8)))
    out = np.empty((B, T, D), dtype=np.float32)
    for c in range(8):
        b, g = c // 4, c % 4
        out[b, 2048 * g:2048 * g + 2048, :] = res.results[c]["out"]
    return out


# version 16
# speedup vs baseline: 1.0602x; 1.0602x over previous
import contextlib
import numpy as np
import ml_dtypes
import concourse.bass as bass
import concourse.mybir as mybir
from concourse.bass_utils import run_bass_kernel_spmd

F32 = mybir.dt.float32
BF16 = mybir.dt.bfloat16
AF = mybir.ActivationFunctionType
ALU = mybir.AluOpType
AX = mybir.AxisListType
f32 = np.float32
DBG = False

B, T, D, H, P = 2, 8192, 1024, 16, 64
TOK = 2048
NHP = 8
DCH = 8
E, FF = 8, 4096
EPS = 1e-5
CSHIFT = 60.0
NTILE = TOK // 128
NSEQ = T // 512
VROW = 65 * 64
TT = 1024


class KB:
    ENG_SEM = {"tensor": "sT", "vector": "sV", "scalar": "sA",
               "gpsimd": "sP", "sync": "sS"}

    def __init__(self, nc, stk):
        self.nc = nc
        self.sems = {}
        for n in ["sT", "sV", "sA", "sP", "sS", "qS", "qG"]:
            self.sems[n] = stk.enter_context(nc.semaphore(n))
        self.cnt = {n: 0 for n in self.sems}
        self.lastw = {}

    def w(self, eng, ek, deps):
        lw = self.lastw.setdefault(ek, {})
        for d in deps:
            if d is None:
                continue
            s, c = d
            if c <= 0 or lw.get(s, 0) >= c:
                continue
            eng.wait_ge(self.sems[s], c)
            lw[s] = c

    def op(self, eng, ek, fn, deps=(), inc=True):
        self.w(eng, ek, deps)
        inst = fn()
        if inc:
            s = self.ENG_SEM[ek]
            self.cnt[s] += 1
            inst.then_inc(self.sems[s], 1)
            return (s, self.cnt[s])
        return None

    def dma(self, eng, ek, q, out, in_, deps=()):
        self.w(eng, ek, deps)
        eng.dma_start(out=out, in_=in_).then_inc(self.sems[q], 16)
        self.cnt[q] += 16
        return (q, self.cnt[q])

    def snap(self):
        return [(n, c) for n, c in self.cnt.items()]


def build():
    nc = bass.Bass("TRN2", target_bir_lowering=False)
    xT = nc.dram_tensor("xT", [D, T], F32, kind="ExternalInput").ap()
    xown = nc.dram_tensor("xown", [TOK, D], F32, kind="ExternalInput").ap()
    wqkv = nc.dram_tensor("wqkv", [D, 768], F32, kind="ExternalInput").ap()
    bqkv = nc.dram_tensor("bqkv", [128, 6], F32, kind="ExternalInput").ap()
    gbc = nc.dram_tensor("gbc", [2, 128, 8], F32, kind="ExternalInput").ap()
    rw_d = nc.dram_tensor("rw_d", [128, 64], F32, kind="ExternalInput").ap()
    w1_d = nc.dram_tensor("w1_d", [E, D, FF], F32, kind="ExternalInput").ap()
    w2_d = nc.dram_tensor("w2_d", [E, FF, D], F32, kind="ExternalInput").ap()
    id32_d = nc.dram_tensor("id32_d", [128, 128], F32, kind="ExternalInput").ap()
    idbf_d = nc.dram_tensor("idbf_d", [128, 128], BF16, kind="ExternalInput").ap()
    out_d = nc.dram_tensor("out", [TOK, D], F32, kind="ExternalOutput").ap()
    akind = {"kind": "ExternalOutput"} if DBG else {}
    attn_scr = nc.dram_tensor("attn_scr", [4, T, P], F32, **akind).ap()
    if DBG:
        dbg_acc1 = nc.dram_tensor("dbg_acc1", [128, NTILE * D], F32, kind="ExternalOutput").ap()
        dbg_xnT = nc.dram_tensor("dbg_xnT", [128, DCH * TOK], BF16, kind="ExternalOutput").ap()
        dbg_wd = nc.dram_tensor("dbg_wd", [128, 128], F32, kind="ExternalOutput").ap()
        dbg_acc2 = nc.dram_tensor("dbg_acc2", [128, NTILE * D], F32, kind="ExternalOutput").ap()

    stk = contextlib.ExitStack()
    with stk:
        kb = KB(nc, stk)
        block = stk.enter_context(nc.Block())
        live, palive = {}, {}

        def alloc(name, shape, dt, side="right"):
            cm = nc.sbuf_tensor(name, shape, dt, side=side)
            t = cm.__enter__()
            live[name] = cm
            return t

        def free(*names):
            for n in names:
                live.pop(n).__exit__(None, None, None)

        def palloc(name, shape, dt):
            cm = nc.psum_tensor(name, shape, dt)
            t = cm.__enter__()
            palive[name] = cm
            return t

        def pfree(*names):
            for n in names:
                palive.pop(n).__exit__(None, None, None)

        def sec(ek, body):
            getattr(block, ek)(body)

        id32 = alloc("id32", [128, 128], F32, side="left")
        idbf = alloc("idbf", [128, 128], BF16, side="left")
        rwTh = alloc("rwTh", [128, 64], BF16, side="left")
        rwTl = alloc("rwTl", [128, 64], BF16, side="left")
        smc = alloc("smc", [128, 44], F32, side="left")

        wq = alloc("wq", [128, DCH * 768], BF16)
        xt2 = alloc("xt2", [128, 2 * DCH * 512], BF16)
        vcs = alloc("vcs", [128, 2 * 512], BF16)
        bqs = alloc("bqs", [128, 6], F32)
        rwTf = alloc("rwTf", [128, 64], F32)
        rwt1 = alloc("rwt1", [128, 64], F32)
        ktp = alloc("ktp", [128, 2 * T], BF16, side="left")
        qtp = alloc("qtp", [128, 2 * T], BF16, side="left")
        vtok = alloc("vtok", [128, 4 * VROW], BF16, side="left")

        psK = palloc("psK", [128, 2 * 512], F32)
        psV = palloc("psV", [128, 2 * 512], F32)
        psQ = palloc("psQ", [128, 2 * 512], F32)
        psVT = palloc("psVT", [128, 1024], BF16)

        tk = {}

        def s_prep(s):
            tk["id32"] = kb.dma(s, "sync", "qS", id32[:, :], id32_d[:, :])
            tk["idbf"] = kb.dma(s, "sync", "qS", idbf[:, :], idbf_d[:, :])
            tk["bqs"] = kb.dma(s, "sync", "qS", bqs[:, :], bqkv[:, :])
            tk["rwTf"] = kb.dma(s, "sync", "qS", rwTf[:, :], rw_d[:, :])
        sec("sync", s_prep)

        def g_prep(g):
            tk["ones"] = kb.op(g, "gpsimd", lambda: g.memset(
                vtok[:, :].rearrange("p (x m) -> p x m", m=65)[:, :, 64:65], 1.0))
            tk["wq"] = kb.dma(g, "gpsimd", "qG",
                              wq[:, :].rearrange("p (c m) -> p c m", c=DCH),
                              wqkv.rearrange("(c p) m -> p c m", p=128))
        sec("gpsimd", g_prep)

        def a_prep(s):
            kb.op(s, "scalar", lambda: s.activation(
                smc[:, 39:40], smc[:, 39:40], AF.Exp,
                bias=f32(0.0), scale=f32(0.0)), inc=True)
        sec("scalar", a_prep)

        def v_prep(v):
            t1 = kb.op(v, "vector", lambda: v.tensor_copy(rwTh[:, :], rwTf[:, :]),
                       deps=[tk["rwTf"]])
            t2 = kb.op(v, "vector", lambda: v.tensor_copy(rwt1[:, :], rwTh[:, :]),
                       deps=[t1])
            t3 = kb.op(v, "vector", lambda: v.tensor_sub(
                rwt1[:, :], rwTf[:, :], rwt1[:, :]), deps=[t2])
            tk["rwT"] = kb.op(v, "vector", lambda: v.tensor_copy(
                rwTl[:, :], rwt1[:, :]), deps=[t3])
        sec("vector", v_prep)

        xt_tok = {}
        xt_free = [None, None]
        psK_free = [None, None]
        psV_free = [None, None]
        psQ_free = [None, None]
        vcs_free = [None, None]
        psVT_free = None

        for tc in range(NSEQ):
            par = tc % 2

            def g_x(g, tc=tc, par=par):
                xt_tok[tc] = kb.dma(
                    g, "gpsimd", "qG",
                    xt2[:, 4096 * par:4096 * par + 4096].rearrange(
                        "p (c m) -> p c m", c=DCH),
                    xT[:, 512 * tc:512 * tc + 512].rearrange(
                        "(c p) m -> p c m", p=128),
                    deps=[xt_free[par]])
            sec("gpsimd", g_x)

            for pair in range(2):
                seq = tc * 2 + pair
                sp = seq % 2

                def t_kvq(t, tc=tc, pair=pair, sp=sp, par=par):
                    for d in range(DCH):
                        tkK = kb.op(t, "tensor", lambda d=d: t.matmul(
                            psK[:, 512 * sp:512 * sp + 512],
                            wq[:, 768 * d + 256 + 128 * pair:
                               768 * d + 256 + 128 * pair + 128],
                            xt2[:, 4096 * par + 512 * d:4096 * par + 512 * d + 512],
                            start=(d == 0), stop=(d == 7)),
                            deps=[xt_tok[tc], tk["wq"], psK_free[sp]],
                            inc=(d == 7))
                    tk["K"] = tkK
                    for d in range(DCH):
                        tkV = kb.op(t, "tensor", lambda d=d: t.matmul(
                            psV[:, 512 * sp:512 * sp + 512],
                            wq[:, 768 * d + 512 + 128 * pair:
                               768 * d + 512 + 128 * pair + 128],
                            xt2[:, 4096 * par + 512 * d:4096 * par + 512 * d + 512],
                            start=(d == 0), stop=(d == 7)),
                            deps=[psV_free[sp]], inc=(d == 7))
                    tk["V"] = tkV
                    for d in range(DCH):
                        tkQ = kb.op(t, "tensor", lambda d=d: t.matmul(
                            psQ[:, 512 * sp:512 * sp + 512],
                            wq[:, 768 * d + 128 * pair:768 * d + 128 * pair + 128],
                            xt2[:, 4096 * par + 512 * d:4096 * par + 512 * d + 512],
                            start=(d == 0), stop=(d == 7)),
                            deps=[psQ_free[sp]], inc=(d == 7))
                    tk["Q"] = tkQ
                sec("tensor", t_kvq)
                if pair == 1:
                    xt_free[par] = tk["Q"]

                def v_kvq(v, tc=tc, pair=pair, sp=sp):
                    tk["Ke"] = kb.op(v, "vector", lambda: v.tensor_scalar(
                        ktp[:, 8192 * pair + 512 * tc:8192 * pair + 512 * tc + 512],
                        psK[:, 512 * sp:512 * sp + 512],
                        bqs[:, 2 + pair:3 + pair], None, ALU.add),
                        deps=[tk["K"], tk["bqs"]])
                    tk["Ve"] = kb.op(v, "vector", lambda: v.tensor_scalar(
                        vcs[:, 512 * sp:512 * sp + 512],
                        psV[:, 512 * sp:512 * sp + 512],
                        bqs[:, 4 + pair:5 + pair], None, ALU.add),
                        deps=[tk["V"], vcs_free[sp]])
                    tk["Qe"] = kb.op(v, "vector", lambda: v.tensor_scalar(
                        qtp[:, 8192 * pair + 512 * tc:8192 * pair + 512 * tc + 512],
                        psQ[:, 512 * sp:512 * sp + 512],
                        bqs[:, pair:pair + 1], None, ALU.add),
                        deps=[tk["Q"], tk["bqs"]])
                sec("vector", v_kvq)
                psK_free[sp] = tk["Ke"]
                psV_free[sp] = tk["Ve"]
                psQ_free[sp] = tk["Qe"]

                def t_vt(t, sp=sp):
                    last = None
                    for j in range(4):
                        last = kb.op(t, "tensor", lambda j=j: t.transpose(
                            psVT[:, 128 * j:128 * j + 128],
                            vcs[:, 512 * sp + 128 * j:512 * sp + 128 * j + 128],
                            idbf[:, :]),
                            deps=[tk["Ve"], psVT_free, tk["idbf"]], inc=(j == 3))
                    tk["vt"] = last
                sec("tensor", t_vt)
                vcs_free[sp] = tk["vt"]

                def v_vt(v, tc=tc, pair=pair):
                    pv = psVT[:, 0:512].rearrange("p (j m) -> p j m", m=128)
                    last = None
                    for half in range(2):
                        hd = 2 * pair + half
                        dst = vtok[:, VROW * hd:VROW * hd + VROW].rearrange(
                            "p (kc m) -> p kc m", m=65)[:, 4 * tc:4 * tc + 4, 0:64]
                        last = kb.op(v, "vector", lambda hd=hd, half=half, dst=dst:
                                     v.tensor_copy(dst, pv[:, :, 64 * half:64 * half + 64]),
                                     deps=[tk["vt"], tk["ones"]], inc=(half == 1))
                    tk["vte"] = last
                sec("vector", v_vt)
                psVT_free = tk["vte"]

        barrierA = kb.snap()
        free("rwt1", "rwTf", "bqs", "vcs", "xt2", "wq")
        pfree("psVT", "psQ", "psV", "psK")

        ssb = alloc("ssb", [128, 2 * 1024], BF16)
        otr = alloc("otr", [128, 2 * 512], F32)
        ostg = alloc("ostg", [128, 4 * 256], F32)

        psS = [palloc("psS0", [128, 1024], F32), palloc("psS1", [128, 1024], F32)]
        psO = palloc("psO", [128, 1024], F32)
        psTR = palloc("psTR", [128, 1024], F32)

        ssb_free0 = [None, None]
        psO_free = None
        psTR_free = [None, None]
        ostg_free = [None, None]

        sT_att0 = kb.cnt["sT"]
        sA_att0 = kb.cnt["sA"]

        def BT(i):
            return sT_att0 + 136 * i

        def p_scorepair(i, kc):
            v = 1 if kc == 0 else 2 if kc == 1 else 2 * kc
            return ("sT", BT(i) + v)

        def p_svh1(i, kc):
            v = 127 if kc == 62 else 128 if kc == 63 else 2 * kc + 3
            return ("sT", BT(i) + v)

        def p_exp(i, kc):
            return ("sA", sA_att0 + 64 * i + kc + 1)

        for pair in range(2):
            def a_pair(s, pair=pair):
                for qp in range(16):
                    i = 16 * pair + qp
                    for kc in range(64):
                        if kc >= 2:
                            war = p_svh1(i, kc - 2)
                        elif i > 0:
                            war = p_svh1(i - 1, 62 if kc == 0 else 63)
                        else:
                            war = None
                        tkE = kb.op(s, "scalar", lambda kc=kc: s.activation(
                            ssb[:, 1024 * (kc % 2):1024 * (kc % 2) + 1024],
                            psS[kc % 2][:, :], AF.Exp,
                            bias=f32(-CSHIFT), scale=f32(1.0)),
                            deps=[p_scorepair(i, kc), war])
                        assert tkE == p_exp(i, kc)
            sec("scalar", a_pair)

            kbase = 8192 * pair
            for qp in range(16):
                i = 16 * pair + qp

                def t_main(t, pair=pair, qp=qp, i=i, kbase=kbase):
                    def scores(kc):
                        for half in range(2):
                            tok = kb.op(t, "tensor",
                                lambda kc=kc, half=half: t.matmul(
                                    psS[kc % 2][:, 512 * half:512 * half + 512],
                                    ktp[64 * half:64 * half + 64,
                                        kbase + 128 * kc:kbase + 128 * kc + 128],
                                    qtp[64 * half:64 * half + 64,
                                        kbase + 512 * qp:kbase + 512 * qp + 512],
                                    start=True, stop=True,
                                    tile_position=(64 * half, 0)),
                                deps=[p_exp(i, kc - 2) if kc >= 2 else
                                      (p_exp(i - 1, 62 if kc % 2 == 0 else 63)
                                       if i > 0 else None)] +
                                     (barrierA if i == 0 else []),
                                inc=(half == 1))
                            if half == 1:
                                assert tok == p_scorepair(i, kc), (i, kc, tok)

                    def sv(kc):
                        for half in range(2):
                            hl = 2 * pair + half
                            tok = kb.op(t, "tensor",
                                lambda kc=kc, half=half, hl=hl: t.matmul(
                                    psO[0:65, 512 * half:512 * half + 512],
                                    vtok[:, VROW * hl + 65 * kc:
                                         VROW * hl + 65 * kc + 65],
                                    ssb[:, 1024 * (kc % 2) + 512 * half:
                                        1024 * (kc % 2) + 512 * half + 512],
                                    start=(kc == 0), stop=(kc == 63)),
                                deps=[p_exp(i, kc),
                                      psO_free if kc == 0 else None],
                                inc=(half == 1))
                            if half == 1:
                                assert tok == p_svh1(i, kc), (i, kc, tok)

                    scores(0)
                    scores(1)
                    for kc2 in range(2, 64, 2):
                        sv(kc2 - 2)
                        scores(kc2)
                        sv(kc2 - 1)
                        scores(kc2 + 1)
                    sv(62)
                    sv(63)
                sec("tensor", t_main)

                def v_oev(v, i=i):
                    tk["oev"] = kb.op(v, "vector", lambda: v.tensor_copy(
                        otr[0:65, :], psO[0:65, :]),
                        deps=[p_svh1(i, 63)])
                sec("vector", v_oev)
                psO_free = tk["oev"]

                for j in range(8):
                    half, t2 = j // 4, j % 4
                    hl = 2 * pair + half
                    osl = 2 * (qp % 2) + half

                    def t_tr(t, half=half, t2=t2, j=j):
                        tk["atr"] = kb.op(t, "tensor", lambda: t.transpose(
                            psTR[:, 512 * (j % 2):512 * (j % 2) + 65],
                            otr[0:65, 512 * half + 128 * t2:
                                512 * half + 128 * t2 + 128],
                            id32[0:65, 0:65]),
                            deps=[tk["oev"], psTR_free[j % 2]])
                    sec("tensor", t_tr)

                    def v_nrm(v, t2=t2, osl=osl, j=j, qp=qp, half=half):
                        rz = kb.op(v, "vector", lambda: v.reciprocal(
                            smc[:, 38:39],
                            psTR[:, 512 * (j % 2) + 64:512 * (j % 2) + 65]),
                            deps=[tk["atr"]])
                        tk["nrm"] = kb.op(v, "vector", lambda: v.tensor_scalar(
                            ostg[:, 256 * osl + 64 * t2:256 * osl + 64 * t2 + 64],
                            psTR[:, 512 * (j % 2):512 * (j % 2) + 64],
                            smc[:, 38:39], None, ALU.mult),
                            deps=[rz, ostg_free[qp % 2] if j == 0 else None])
                    sec("vector", v_nrm)
                    psTR_free[j % 2] = tk["nrm"]

                    if t2 == 3:
                        def s_ost(s, hl=hl, qp=qp, osl=osl):
                            ostg_free[qp % 2] = kb.dma(
                                s, "sync", "qS",
                                attn_scr[hl, 512 * qp:512 * qp + 512, :].rearrange(
                                    "(t p) m -> p t m", p=128),
                                ostg[:, 256 * osl:256 * osl + 256].rearrange(
                                    "p (t m) -> p t m", m=64),
                                deps=[tk["nrm"]])
                        sec("sync", s_ost)
                assert kb.cnt["sT"] == BT(i) + 136, (i, kb.cnt["sT"])

        barrierB = kb.snap()
        free("ostg", "otr", "ssb")
        free("vtok", "qtp", "ktp")
        pfree("psTR", "psO", "psS1", "psS0")

        acc = alloc("acc", [128, NTILE * D], F32, side="left")
        xnT = alloc("xnT", [128, DCH * TOK], BF16, side="left")
        wd = alloc("wd", [128, 128], F32, side="left")
        gbs = alloc("gbs", [128, 16], F32)
        xti = alloc("xti", [128, 2 * D], F32)
        ati = alloc("ati", [128, 2 * D], F32)
        v1t = alloc("v1t", [128, 2 * D], F32)
        xct = alloc("xct", [128, 2 * D], F32)
        x1t = alloc("x1t", [128, 2 * D], F32)
        xwt = alloc("xwt", [128, 2 * D], F32)
        sqd = alloc("sqd", [128, D], F32)
        lgb = alloc("lgb", [128, 2 * 32], F32)

        psL = palloc("psL", [128, 512], F32)
        psX = palloc("psX", [128, 1024], F32)

        def s_gb(s):
            tk["gb0"] = kb.dma(s, "sync", "qS", gbs[:, 0:8], gbc[0, :, :],
                               deps=barrierB)
            tk["gb1"] = kb.dma(s, "sync", "qS", gbs[:, 8:16], gbc[1, :, :])
        sec("sync", s_gb)

        xld = {}
        ald = {}

        sVC0 = kb.cnt["sV"]
        sAC0 = kb.cnt["sA"]
        sTC0 = kb.cnt["sT"]

        def pV(t0, k):
            return ("sV", sVC0 + 24 * t0 + k)

        def pA(t0, k):
            return ("sA", sAC0 + 7 * t0 + k)

        def pT(t0, k):
            return ("sT", sTC0 + 2 * t0 + k)

        def a_c(s):
            for t0 in range(NTILE):
                pt = t0 % 2
                c1, c2 = 12 * pt, 12 * pt + 6
                xc = xct[:, D * pt:D * pt + D]
                for (cc, src_k) in ((c1, 4), (c2, 9)):
                    tok = kb.op(s, "scalar", lambda cc=cc, xc=xc: s.activation(
                        sqd[:, :], xc, AF.Square,
                        accum_out=smc[:, cc + 2:cc + 3]), deps=[pV(t0, src_k)])
                    tok = kb.op(s, "scalar", lambda cc=cc: s.activation(
                        smc[:, cc + 4:cc + 5], smc[:, cc + 2:cc + 3], AF.Ln,
                        bias=f32(EPS), scale=f32(1.0 / D)), deps=[tok])
                    tok = kb.op(s, "scalar", lambda cc=cc: s.activation(
                        smc[:, cc + 5:cc + 6], smc[:, cc + 4:cc + 5], AF.Exp,
                        bias=f32(0.0), scale=f32(-0.5)), deps=[tok])
                cr = 24 + 4 * pt
                tok = kb.op(s, "scalar", lambda pt=pt, cr=cr: s.activation(
                    lgb[:, 32 * pt + 8:32 * pt + 16], lgb[:, 32 * pt:32 * pt + 8],
                    AF.Exp, bias=smc[:, cr + 1:cr + 2], scale=f32(1.0 / 1.5)),
                    deps=[pV(t0, 14)])
                assert tok == pA(t0, 7)
        sec("scalar", a_c)

        def s_x(s, t0):
            xld[t0] = kb.dma(s, "sync", "qS",
                             xti[:, D * (t0 % 2):D * (t0 % 2) + D],
                             xown[128 * t0:128 * t0 + 128, :],
                             deps=(barrierB if t0 < 2 else [pV(t0 - 2, 4)]))

        def s_at(s, t0):
            hl, a0 = t0 // 4, 128 * (t0 % 4)
            src = attn_scr[hl].rearrange("(a r) m -> a (r m)", r=16)[a0:a0 + 128, :]
            ald[t0] = kb.dma(s, "sync", "qS",
                             ati[:, D * (t0 % 2):D * (t0 % 2) + D], src,
                             deps=(barrierB if t0 < 2 else [pV(t0 - 2, 1)]))
        sec("sync", lambda s: s_x(s, 0))
        sec("sync", lambda s: s_at(s, 0))
        sec("sync", lambda s: s_x(s, 1))
        sec("sync", lambda s: s_at(s, 1))

        for t0 in range(NTILE):
            pt = t0 % 2
            c1, c2 = 12 * pt, 12 * pt + 6
            cr = 24 + 4 * pt
            xsl = xti[:, D * pt:D * pt + D]
            asl = ati[:, D * pt:D * pt + D]
            v1 = v1t[:, D * pt:D * pt + D]
            xc = xct[:, D * pt:D * pt + D]
            x1 = x1t[:, D * pt:D * pt + D]
            xw = xwt[:, D * pt:D * pt + D]
            lgs = lgb[:, 32 * pt:32 * pt + 32]

            def v_a(v, t0=t0, pt=pt, c1=c1, c2=c2):
                tok = kb.op(v, "vector", lambda: v.tensor_add(v1, xsl, asl),
                            deps=[xld[t0], ald[t0]] +
                                 ([pV(t0 - 2, 4)] if t0 >= 2 else []))
                assert tok == pV(t0, 1)
                kb.op(v, "vector", lambda: v.reduce_sum(
                    smc[:, c1:c1 + 1], v1, axis=AX.X), deps=[tok])
                kb.op(v, "vector", lambda: v.tensor_scalar(
                    smc[:, c1 + 1:c1 + 2], smc[:, c1:c1 + 1],
                    -1.0 / D, None, ALU.mult), deps=[pV(t0, 2)])
                kb.op(v, "vector", lambda: v.tensor_scalar(
                    xc, v1, smc[:, c1 + 1:c1 + 2], None, ALU.add),
                    deps=[pV(t0, 3)] + ([pA(t0 - 2, 4)] if t0 >= 2 else []))
                kb.op(v, "vector", lambda: v.tensor_scalar(
                    x1, xc, smc[:, c1 + 5:c1 + 6], None, ALU.mult),
                    deps=[pA(t0, 3)])
                kb.op(v, "vector", lambda t0=t0: v.tensor_scalar(
                    acc[:, 1024 * t0:1024 * t0 + 1024], x1,
                    2.0, None, ALU.mult), deps=[pV(t0, 5)])
                kb.op(v, "vector", lambda: v.reduce_sum(
                    smc[:, c2:c2 + 1], x1, axis=AX.X), deps=[pV(t0, 5)])
                kb.op(v, "vector", lambda: v.tensor_scalar(
                    smc[:, c2 + 1:c2 + 2], smc[:, c2:c2 + 1],
                    -1.0 / D, None, ALU.mult), deps=[pV(t0, 7)])
                kb.op(v, "vector", lambda: v.tensor_scalar(
                    xc, x1, smc[:, c2 + 1:c2 + 2], None, ALU.add),
                    deps=[pV(t0, 8), pA(t0, 1)])
                kb.op(v, "vector", lambda: v.tensor_scalar(
                    xw, xc, smc[:, c2 + 5:c2 + 6], None, ALU.mult),
                    deps=[pA(t0, 6)] + ([pT(t0 - 2, 1)] if t0 >= 2 else []))
            sec("vector", v_a)

            def t_xnt(t, t0=t0, xw=xw):
                last = None
                for d in range(DCH):
                    last = kb.op(t, "tensor", lambda d=d: t.transpose(
                        psX[:, 128 * d:128 * d + 128],
                        xw[:, 128 * d:128 * d + 128], id32[:, :]),
                        deps=[pV(t0, 10)] +
                             ([pV(t0 - 1, 11)] if t0 >= 1 else []),
                        inc=(d == 7))
                assert last == pT(t0, 1)
            sec("tensor", t_xnt)

            def v_b(v, t0=t0):
                last = None
                for d in range(DCH):
                    last = kb.op(v, "vector", lambda d=d: v.tensor_scalar(
                        xnT[:, 2048 * d + 128 * t0:2048 * d + 128 * t0 + 128],
                        psX[:, 128 * d:128 * d + 128],
                        gbs[:, d:d + 1], gbs[:, 8 + d:9 + d],
                        ALU.mult, ALU.add),
                        deps=[pT(t0, 1), tk["gb0"], tk["gb1"]], inc=(d == 7))
                assert last == pV(t0, 11)
            sec("vector", v_b)

            def t_rt(t, t0=t0):
                n = 0
                for d in range(DCH):
                    for hl2 in range(2):
                        rsrc = rwTh if hl2 == 0 else rwTl
                        last = kb.op(t, "tensor", lambda d=d, rsrc=rsrc, n=n:
                                     t.matmul(
                                         psL[:, 0:8],
                                         xnT[:, 2048 * d + 128 * t0:
                                             2048 * d + 128 * t0 + 128],
                                         rsrc[:, 8 * d:8 * d + 8],
                                         start=(n == 0), stop=(n == 15)),
                                     deps=[pV(t0, 11), tk["rwT"]] +
                                          ([pV(t0 - 1, 12)] if t0 >= 1 else []),
                                     inc=(n == 15))
                        n += 1
                assert last == pT(t0, 2)
            sec("tensor", t_rt)

            def v_c(v, t0=t0, pt=pt, cr=cr, lgs=lgs):
                tok = kb.op(v, "vector", lambda: v.tensor_copy(
                    lgs[:, 0:8], psL[:, 0:8]), deps=[pT(t0, 2)])
                assert tok == pV(t0, 12)
                kb.op(v, "vector", lambda: v.reduce_max(
                    smc[:, cr:cr + 1], lgs[:, 0:8], axis=AX.X), deps=[tok])
                kb.op(v, "vector", lambda: v.tensor_scalar(
                    smc[:, cr + 1:cr + 2], smc[:, cr:cr + 1],
                    -1.0 / 1.5, None, ALU.mult), deps=[pV(t0, 13)])
                kb.op(v, "vector", lambda: v.reduce_sum(
                    smc[:, cr + 2:cr + 3], lgs[:, 8:16], axis=AX.X),
                    deps=[pA(t0, 7)])
                kb.op(v, "vector", lambda: v.reciprocal(
                    smc[:, cr + 3:cr + 4], smc[:, cr + 2:cr + 3]),
                    deps=[pV(t0, 15)])
                kb.op(v, "vector", lambda: v.tensor_scalar(
                    lgs[:, 16:24], lgs[:, 8:16], smc[:, cr + 3:cr + 4],
                    None, ALU.mult), deps=[pV(t0, 16)])
                kb.op(v, "vector", lambda: v.reduce_max(
                    smc[:, cr + 1:cr + 2], lgs[:, 16:24], axis=AX.X),
                    deps=[pV(t0, 17)])
                kb.op(v, "vector", lambda: v.tensor_scalar(
                    lgs[:, 24:32], lgs[:, 16:24], smc[:, cr + 1:cr + 2],
                    None, ALU.is_equal), deps=[pV(t0, 18)])
                kb.op(v, "vector", lambda: v.tensor_scalar(
                    lgs[:, 24:32], lgs[:, 24:32], 1e30, None, ALU.mult),
                    deps=[pV(t0, 19)])
                kb.op(v, "vector", lambda: v.tensor_sub(
                    lgs[:, 24:32], lgs[:, 16:24], lgs[:, 24:32]),
                    deps=[pV(t0, 20)])
                kb.op(v, "vector", lambda: v.reduce_max(
                    smc[:, cr + 1:cr + 2], lgs[:, 24:32], axis=AX.X),
                    deps=[pV(t0, 21)])
                kb.op(v, "vector", lambda: v.tensor_scalar(
                    lgs[:, 24:32], lgs[:, 16:24], smc[:, cr + 1:cr + 2],
                    None, ALU.is_ge), deps=[pV(t0, 22)])
                tok = kb.op(v, "vector", lambda t0=t0: v.tensor_mul(
                    wd[:, 8 * t0:8 * t0 + 8], lgs[:, 16:24], lgs[:, 24:32]),
                    deps=[pV(t0, 23)])
                assert tok == pV(t0, 24)
            sec("vector", v_c)

            if t0 + 2 < NTILE:
                sec("sync", lambda s, t0=t0: s_x(s, t0 + 2))
                sec("sync", lambda s, t0=t0: s_at(s, t0 + 2))

        barrierC = kb.snap()
        if DBG:
            def s_dc(s):
                kb.dma(s, "sync", "qS", dbg_acc1, acc[:, :], deps=barrierC)
                kb.dma(s, "sync", "qS", dbg_xnT, xnT[:, :])
                kb.dma(s, "sync", "qS", dbg_wd, wd[:, :])
            sec("sync", s_dc)
            barrierC = kb.snap()
        free("lgb", "sqd", "xwt", "x1t", "xct", "v1t", "ati", "xti", "gbs")
        pfree("psX", "psL")

        w1s = alloc("w1s", [128, 2 * DCH * 256], BF16)
        w2s = alloc("w2s", [128, 2 * 32 * 128], BF16)
        hT = alloc("hT", [128, 32 * TT], BF16)
        ysc = alloc("ysc", [128, 2 * 128], F32)

        psh = palloc("psh", [128, 4 * 512], F32)
        psy = palloc("psy", [128, 2 * 512], F32)

        NSLAB = 2 * E * 16
        NQ = 2 * E * 8
        w1_tok = {}
        w2_tok = {}
        acc_tok = {}

        sTD0 = kb.cnt["sT"]
        sAD0 = kb.cnt["sA"]
        sVD0 = kb.cnt["sV"]

        def sT_base(eidx):
            return sTD0 + 128 * eidx

        def sA_base(eidx):
            return sAD0 + 128 * eidx + (24 if eidx > 7 else 0)

        def sV_base(eidx):
            return sVD0 + 64 * eidx + (32 if eidx > 7 else 0)

        def silu_last_of_slab(gi):
            eidx, fs = gi // 16, gi % 16
            return ("sA", sA_base(eidx) + 4 * (fs + 1))

        def y_last_of_quarter(qi):
            eidx, dq = qi // 8, qi % 8
            return ("sT", sT_base(eidx) + 64 + 8 * (dq + 1))

        def issue_weights(g, eidx):
            if eidx >= 16:
                return
            for fs in range(16):
                gi = eidx * 16 + fs

                def do1(gi=gi, e=(gi // 16) % E if False else None):
                    pass
                e_ = ((gi // 16) % E) if False else ((gi % 256) // 16) % E
                e_ = (gi // 16) % E
                w1_tok[gi] = kb.dma(
                    g, "gpsimd", "qG",
                    w1s[:, 2048 * (gi % 2):2048 * (gi % 2) + 2048].rearrange(
                        "p (c m) -> p c m", c=DCH),
                    w1_d[e_, :, 256 * fs:256 * fs + 256].rearrange(
                        "(c p) m -> p c m", p=128),
                    deps=[silu_last_of_slab(gi - 2) if gi >= 2 else None] +
                         (barrierC if gi < 2 else []))
            for dq in range(8):
                qi = eidx * 8 + dq
                e_ = (qi // 8) % E
                w2_tok[qi] = kb.dma(
                    g, "gpsimd", "qG",
                    w2s[:, 4096 * (qi % 2):4096 * (qi % 2) + 4096].rearrange(
                        "p (c m) -> p c m", c=32),
                    w2_d[e_, :, 128 * dq:128 * dq + 128].rearrange(
                        "(c p) m -> p c m", p=128),
                    deps=[y_last_of_quarter(qi - 2) if qi >= 2 else None] +
                         (barrierC if qi < 2 else []))

        sec("gpsimd", lambda g: issue_weights(g, 0))
        sec("gpsimd", lambda g: issue_weights(g, 1))

        xct2 = alloc("xct2", [128, D], F32)
        sqd2 = alloc("sqd2", [128, D], F32)
        ot = alloc("ot", [128, 2 * D], F32)
        e_state = {"echain": None, "st_free": [None, None], "final": None}

        def emit_E_half(h):
            for t0 in range(8 * h, 8 * h + 8):
                asl = acc[:, 1024 * t0:1024 * t0 + 1024]
                toks = {}

                def v_e1(v, t0=t0, asl=asl):
                    d0 = [e_state["echain"]] + \
                         [acc_tok[(t0, dqq)] for dqq in range(8)]
                    toks["s"] = kb.op(v, "vector", lambda: v.reduce_sum(
                        smc[:, 32:33], asl, axis=AX.X), deps=d0)
                    toks["nm"] = kb.op(v, "vector", lambda: v.tensor_scalar(
                        smc[:, 33:34], smc[:, 32:33], -1.0 / D, None, ALU.mult),
                        deps=[toks["s"]])
                    toks["xc"] = kb.op(v, "vector", lambda: v.tensor_scalar(
                        xct2[:, :], asl, smc[:, 33:34], None, ALU.add),
                        deps=[toks["nm"]])
                sec("vector", v_e1)

                def a_e(s, t0=t0):
                    toks["sq"] = kb.op(s, "scalar", lambda: s.activation(
                        sqd2[:, :], xct2[:, :], AF.Square,
                        accum_out=smc[:, 34:35]), deps=[toks["xc"]])
                    toks["ln"] = kb.op(s, "scalar", lambda: s.activation(
                        smc[:, 35:36], smc[:, 34:35], AF.Ln,
                        bias=f32(EPS), scale=f32(1.0 / D)), deps=[toks["sq"]])
                    toks["rstd"] = kb.op(s, "scalar", lambda: s.activation(
                        smc[:, 36:37], smc[:, 35:36], AF.Exp,
                        bias=f32(0.0), scale=f32(-0.5)), deps=[toks["ln"]])
                sec("scalar", a_e)

                def v_e2(v, t0=t0):
                    toks["o"] = kb.op(v, "vector", lambda: v.tensor_scalar(
                        ot[:, D * (t0 % 2):D * (t0 % 2) + D], xct2[:, :],
                        smc[:, 36:37], None, ALU.mult),
                        deps=[toks["rstd"], e_state["st_free"][t0 % 2]])
                sec("vector", v_e2)
                e_state["echain"] = toks["o"]

                def s_o(s, t0=t0):
                    tokd = kb.dma(s, "sync", "qS",
                                  out_d[128 * t0:128 * t0 + 128, :],
                                  ot[:, D * (t0 % 2):D * (t0 % 2) + D],
                                  deps=[toks["o"]])
                    e_state["st_free"][t0 % 2] = tokd
                    e_state["final"] = tokd
                sec("sync", s_o)

        for tt in range(2):
            for e in range(E):
                eidx = tt * E + e
                sT0 = kb.cnt["sT"]
                sA0 = kb.cnt["sA"]
                sV0 = kb.cnt["sV"]
                assert sT0 == sT_base(eidx) and sA0 == sA_base(eidx) \
                    and sV0 == sV_base(eidx), (eidx, sT0, sA0, sV0)

                if eidx + 2 < 16:
                    sec("gpsimd", lambda g, eidx=eidx: issue_weights(g, eidx + 2))

                def a_blk(s, tt=tt, e=e, sT0=sT0, sV0=sV0):
                    for grp in range(64):
                        fs, fc, sub = grp // 4, (grp % 4) // 2, grp % 2
                        fcg = 2 * fs + fc
                        slot = grp % 4
                        kb.op(s, "scalar", lambda fcg=fcg, sub=sub, slot=slot:
                              s.activation(
                                  hT[:, 1024 * fcg + 512 * sub:
                                     1024 * fcg + 512 * sub + 512],
                                  psh[:, 512 * slot:512 * slot + 512], AF.Silu),
                              deps=[("sT", sT0 + grp + 1), ("sT", sT0)])
                    for k in range(64):
                        dq, tkc = k // 8, k % 8
                        ti = 8 * tt + tkc
                        slot = k % 2
                        kb.op(s, "scalar", lambda ti=ti, e=e, slot=slot, k=k:
                              s.activation(
                                  ysc[:, 128 * slot:128 * slot + 128],
                                  psy[:, 512 * slot:512 * slot + 128], AF.Copy,
                                  bias=0.0, scale=wd[:, 8 * ti + e:8 * ti + e + 1]),
                              deps=[("sT", sT0 + 64 + k + 1),
                                    ("sV", sV0 + k - 1) if k >= 2 else ("sV", sV0)])
                sec("scalar", a_blk)
                assert kb.cnt["sA"] == sA0 + 128

                def t_blk(t, tt=tt, e=e, eidx=eidx, sA0=sA0):
                    for grp in range(64):
                        fs, fc, sub = grp // 4, (grp % 4) // 2, grp % 2
                        gi = eidx * 16 + fs
                        slot = grp % 4
                        for d in range(DCH):
                            kb.op(t, "tensor", lambda gi=gi, fc=fc, sub=sub,
                                  slot=slot, d=d, tt=tt: t.matmul(
                                      psh[:, 512 * slot:512 * slot + 512],
                                      w1s[:, 2048 * (gi % 2) + 256 * d + 128 * fc:
                                          2048 * (gi % 2) + 256 * d + 128 * fc + 128],
                                      xnT[:, 2048 * d + 1024 * tt + 512 * sub:
                                          2048 * d + 1024 * tt + 512 * sub + 512],
                                      start=(d == 0), stop=(d == 7)),
                                  deps=[w1_tok[gi],
                                        ("sA", sA0 + grp - 3) if grp >= 4 else None],
                                  inc=(d == 7))
                    for k in range(64):
                        dq, tkc = k // 8, k % 8
                        qi = eidx * 8 + dq
                        slot = k % 2
                        for fcg in range(32):
                            kb.op(t, "tensor", lambda qi=qi, tkc=tkc, slot=slot,
                                  fcg=fcg: t.matmul(
                                      psy[:, 512 * slot:512 * slot + 128],
                                      hT[:, 1024 * fcg + 128 * tkc:
                                         1024 * fcg + 128 * tkc + 128],
                                      w2s[:, 4096 * (qi % 2) + 128 * fcg:
                                          4096 * (qi % 2) + 128 * fcg + 128],
                                      start=(fcg == 0), stop=(fcg == 31)),
                                  deps=[w2_tok[qi], ("sA", sA0 + 64),
                                        ("sA", sA0 + 64 + k - 1) if k >= 2 else None],
                                  inc=(fcg == 31))
                sec("tensor", t_blk)
                assert kb.cnt["sT"] == sT0 + 128

                def v_blk(v, tt=tt, sA0=sA0):
                    for k in range(64):
                        dq, tkc = k // 8, k % 8
                        ti = 8 * tt + tkc
                        slot = k % 2
                        a0 = 1024 * ti + 128 * dq
                        prev = acc_tok.get((ti, dq))
                        acc_tok[(ti, dq)] = kb.op(
                            v, "vector", lambda a0=a0, slot=slot: v.tensor_add(
                                acc[:, a0:a0 + 128], acc[:, a0:a0 + 128],
                                ysc[:, 128 * slot:128 * slot + 128]),
                            deps=[("sA", sA0 + 64 + k + 1), prev] +
                                 ([] if prev else barrierC))
                sec("vector", v_blk)
                assert kb.cnt["sV"] == sV0 + 64
                if eidx == 7:
                    emit_E_half(0)
                elif eidx == 15:
                    emit_E_half(1)

        barrierD = kb.snap()
        if DBG:
            sec("sync", lambda s: kb.dma(s, "sync", "qS", dbg_acc2, acc[:, :],
                                         deps=barrierD))
            barrierD = kb.snap()

        def s_fin(s):
            kb.w(s, "sync", [e_state["final"]] + barrierD)
        sec("sync", s_fin)

        free("ot", "sqd2", "xct2")
        free("ysc", "hT", "w2s", "w1s")
        free("wd", "xnT")
        pfree("psy", "psh")
        free("acc")
        free("smc", "rwTl", "rwTh", "idbf", "id32")

    return nc


_NC_CACHE = {}


def kernel(L0_attn_x, L0_attn_W_qkv, L0_attn_b_qkv, L0_ffn_gamma, L0_ffn_beta,
           L0_ffn_router_w, L0_ffn_W1, L0_ffn_W2):
    x = np.ascontiguousarray(np.asarray(L0_attn_x, dtype=np.float32))
    wqkv_full = np.asarray(L0_attn_W_qkv, dtype=np.float32)
    bq_full = np.asarray(L0_attn_b_qkv, dtype=np.float32)
    gamma = np.asarray(L0_ffn_gamma, dtype=np.float32)
    beta = np.asarray(L0_ffn_beta, dtype=np.float32)
    rw = np.asarray(L0_ffn_router_w, dtype=np.float32)
    rwt = np.ascontiguousarray(
        rw.reshape(8, 8, 128).transpose(2, 1, 0).reshape(128, 64))
    w1 = np.ascontiguousarray(np.asarray(L0_ffn_W1, dtype=np.float32))
    w2 = np.ascontiguousarray(np.asarray(L0_ffn_W2, dtype=np.float32))

    gbc = np.ascontiguousarray(np.stack([
        gamma.reshape(8, 128).T, beta.reshape(8, 128).T]))
    id32 = np.eye(128, dtype=np.float32)
    idbf = np.eye(128, dtype=ml_dtypes.bfloat16)

    if "nc" not in _NC_CACHE:
        _NC_CACHE["nc"] = build()
    nc = _NC_CACHE["nc"]

    in_maps = []
    xTb = [np.ascontiguousarray(x[b].T) for b in range(B)]
    for c in range(8):
        b, g = c // 4, c % 4
        cols = list(range(256 * g, 256 * g + 256)) + \
               list(range(1024 + 256 * g, 1024 + 256 * g + 256)) + \
               list(range(2048 + 256 * g, 2048 + 256 * g + 256))
        wq_c = np.ascontiguousarray(wqkv_full[:, cols])
        bq_c = np.ascontiguousarray(
            bq_full[cols].reshape(6, 128).T)
        in_maps.append({
            "xT": xTb[b],
            "xown": np.ascontiguousarray(x[b, 2048 * g:2048 * g + 2048, :]),
            "wqkv": wq_c, "bqkv": bq_c, "gbc": gbc, "rw_d": rwt,
            "w1_d": w1, "w2_d": w2, "id32_d": id32, "idbf_d": idbf,
        })
    res = run_bass_kernel_spmd(nc, in_maps, core_ids=list(range(8)))
    out = np.empty((B, T, D), dtype=np.float32)
    for c in range(8):
        b, g = c // 4, c % 4
        out[b, 2048 * g:2048 * g + 2048, :] = res.results[c]["out"]
    return out
